# version 02
# baseline (speedup 1.0000x reference)
import numpy as np
import concourse.bass as bass
import concourse.mybir as mybir
from concourse.bass_utils import run_bass_kernel_spmd
from contextlib import ExitStack

F32 = mybir.dt.float32
BF16 = mybir.dt.bfloat16
F16 = mybir.dt.float16
I16 = mybir.dt.int16
U8 = mybir.dt.uint8
ALU = mybir.AluOpType
AF = mybir.ActivationFunctionType
AX = mybir.AxisListType.X

S = 2048
DM = 4096
D = 128
HPC = 16
SCALE = 1.0 / float(np.sqrt(128.0))
EXPB = -20.0
EPS = 1e-5
VIS_LO, VIS_HI = 1, 513
TXT_LO, TXT_HI = 513, 1992


class Sem:
    __slots__ = ("h", "n", "name")

    def __init__(self, h, name):
        self.h = h
        self.n = 0
        self.name = name


class Prog:
    ENGINES = ("tensor", "vector", "scalar", "gpsimd", "sync")

    def __init__(self, nc, stack):
        self.nc = nc
        self.stack = stack
        self.ops = {e: [] for e in self.ENGINES}
        self.last = {e: None for e in self.ENGINES}
        self.auto_serial = {}
        self._serial_prev = {}

    def sem(self, name):
        return Sem(self.stack.enter_context(self.nc.semaphore(name)), name)

    def op(self, eng, fn, waits=(), inc=None, inck=1, is_dma=False):
        waits = tuple((s, v) for (s, v) in waits if v is not None)
        ser = self.auto_serial.get(eng)
        if ser is not None:
            sp = self._serial_prev.get(eng)
            if sp is not None:
                waits = waits + (sp,)
            if not is_dma and inc is None:
                inc = ser
        ev = None
        if inc is not None:
            inc.n += inck
            ev = inc.n
            self.last[eng] = (inc, ev)
            if ser is not None and not is_dma:
                self._serial_prev[eng] = (inc, ev)
        self.ops[eng].append((fn, waits, inc, inck))
        return ev

    def dma(self, q, qsem, out, in_, waits=(), **kw):
        def fn(e, out=out, in_=in_, kw=kw):
            return e.dma_start(out=out, in_=in_, **kw)
        return self.op(q, fn, waits, inc=qsem, inck=16, is_dma=True)

    def barrier_waits(self):
        return tuple(self.last[e] for e in self.ENGINES if self.last[e])

    def emit_phase(self, block):
        for eng in self.ENGINES:
            ops = self.ops[eng]
            if not ops:
                continue
            getattr(block, eng)(self._mk_body(ops))
        self.ops = {e: [] for e in self.ENGINES}

    def _mk_body(self, ops):
        def body(e):
            waited = {}
            for (fn, waits, inc, inck) in ops:
                for (s, v) in waits:
                    if waited.get(id(s), -1) >= v:
                        continue
                    e.wait_ge(s.h, v)
                    waited[id(s)] = v
                inst = fn(e)
                if inc is not None:
                    inst.then_inc(inc.h, inck)
        return body


def build(dbg=False):
    nc = bass.Bass("TRN2", target_bir_lowering=False,
                   detect_race_conditions=False)

    xT_in = nc.dram_tensor("xT", [DM, S], F32, kind="ExternalInput").ap()
    xres_in = nc.dram_tensor("xres", [1024, DM], F32, kind="ExternalInput").ap()
    qwT_in = nc.dram_tensor("qwT", [DM, 2048], F32, kind="ExternalInput").ap()
    kwT_in = nc.dram_tensor("kwT", [DM, 2048], F32, kind="ExternalInput").ap()
    vwT_in = nc.dram_tensor("vwT", [DM, 2048], F32, kind="ExternalInput").ap()
    owT_in = nc.dram_tensor("owT", [2048, DM], F32, kind="ExternalInput").ap()
    pkT_in = nc.dram_tensor("pkT", [2048, DM], I16, kind="ExternalInput").ap()
    scP_in = nc.dram_tensor("scP", [2048, DM], F16, kind="ExternalInput").ap()
    mkP_in = nc.dram_tensor("mkP", [2048, DM], U8, kind="ExternalInput").ap()
    biasr_in = nc.dram_tensor("biasr", [128, DM], F32, kind="ExternalInput").ap()
    cosD_in = nc.dram_tensor("cosD", [128, S], F32, kind="ExternalInput").ap()
    sinD_in = nc.dram_tensor("sinD", [128, S], F32, kind="ExternalInput").ap()
    flag_in = nc.dram_tensor("flag", [1, 1], F32, kind="ExternalInput").ap()
    out_ext = nc.dram_tensor("out", [1024, DM], F32, kind="ExternalOutput").ap()

    qT_d = nc.dram_tensor("qT_d", [HPC, 128, S], BF16).ap()
    kT_d = nc.dram_tensor("kT_d", [HPC, 128, S], BF16).ap()
    v_d = nc.dram_tensor("v_d", [S, 2048], BF16).ap()
    wdq_d = nc.dram_tensor("wdq_d", [32, 128, DM], BF16).ap()
    rsi_d = nc.dram_tensor("rsi_d", [S, DM], BF16).ap()
    rso_d = nc.dram_tensor("rso_d", [1024, DM], BF16).ap()
    impi_d = nc.dram_tensor("impi_d", [1, 512], F32).ap()
    impo_d = nc.dram_tensor("impo_d", [1, 512], F32).ap()
    mask_d = nc.dram_tensor("mask_d", [1, 768], F32).ap()
    dbgkind = "ExternalOutput" if dbg == 2 else "Internal"
    dbg1_d = nc.dram_tensor("dbg1_d", [128, 8 * DM], BF16, kind=dbgkind).ap() \
        if dbg else None
    dbg2_d = nc.dram_tensor("dbg2_d", [128, 32 * 512], BF16, kind=dbgkind).ap() \
        if dbg else None
    dbg3_d = nc.dram_tensor("dbg3_d", [128, 8 * DM], BF16, kind=dbgkind).ap() \
        if dbg else None
    if dbg == 2:
        dbg_qT = nc.dram_tensor("dbg_qT", [HPC, 128, S], BF16, kind=dbgkind).ap()
        dbg_kT = nc.dram_tensor("dbg_kT", [HPC, 128, S], BF16, kind=dbgkind).ap()
        dbg_v = nc.dram_tensor("dbg_v", [S, 2048], BF16, kind=dbgkind).ap()
        dbg_w = nc.dram_tensor("dbg_w", [32, 128, DM], BF16, kind=dbgkind).ap()
        dbg_rsi = nc.dram_tensor("dbg_rsi", [S, DM], BF16, kind=dbgkind).ap()
        dbg_rso = nc.dram_tensor("dbg_rso", [1024, DM], BF16, kind=dbgkind).ap()
        dbg_msk = nc.dram_tensor("dbg_msk", [1, 768], F32, kind=dbgkind).ap()

    pairs = [[0, 1], [2, 3], [4, 5], [6, 7]]
    allg = [[0, 1, 2, 3, 4, 5, 6, 7]]

    with ExitStack() as stack:
        P = Prog(nc, stack)
        qg = P.sem("qg")
        qs = P.sem("qs")
        qa = P.sem("qa")
        ccol = P.sem("ccol")
        ts = P.sem("ts")
        vs = P.sem("vs")
        ss = P.sem("ss")
        gs = P.sem("gs")
        P.auto_serial = {"vector": vs, "scalar": ss}

        PB = [stack.enter_context(nc.psum_tensor(f"pb{i}", [128, 512], F32))
              for i in range(7)]
        PBt = stack.enter_context(nc.psum_tensor("pbt", [128, 1024], BF16))

        sb = lambda st_, n, shp, dt: st_.enter_context(nc.sbuf_tensor(n, shp, dt))
        ident = sb(stack, "ident", [128, 128], BF16)
        tri = sb(stack, "tri", [128, 128], BF16)
        maskb = sb(stack, "maskb", [128, 8], F32)
        flag_sb = sb(stack, "flag_sb", [1, 1], F32)
        ones_r = sb(stack, "ones_r", [1, 128], F32)
        one1 = sb(stack, "one1", [1, 1], F32)
        stat = sb(stack, "stat", [128, 16], F32)
        b20 = sb(stack, "b20", [128, 1], F32)
        beps = sb(stack, "beps", [128, 1], F32)

        block = stack.enter_context(nc.Block())

        ev_flag = P.dma("sync", qs, flag_sb[:], flag_in[:])
        e1 = P.op("gpsimd", lambda g: g.memset(ident[:], 0.0), inc=gs)
        e2 = P.op("gpsimd", lambda g: g.affine_select(
            out=ident[:], in_=ident[:], compare_op=ALU.not_equal, fill=1.0,
            base=0, pattern=[[-1, 128]], channel_multiplier=1),
            waits=[(gs, e1)], inc=gs)
        e3 = P.op("gpsimd", lambda g: g.memset(tri[:], 1.0),
                  waits=[(gs, e2)], inc=gs)
        ev_tri = P.op("gpsimd", lambda g: g.affine_select(
            out=tri[:], in_=tri[:], compare_op=ALU.is_ge, fill=0.0,
            base=0, pattern=[[1, 128]], channel_multiplier=-1),
            waits=[(gs, e3)], inc=gs)
        P.op("vector", lambda v: v.memset(ones_r[:], 1.0))
        P.op("vector", lambda v: v.memset(b20[:], EXPB))
        P.op("vector", lambda v: v.memset(beps[:], EPS))
        ev_one = P.op("vector", lambda v: v.memset(one1[:], 1.0), inc=vs)
        P.emit_phase(block)

        MASK_READY = None
        with ExitStack() as s1:
            xT = sb(s1, "xT_sb", [128, 32 * S], BF16)
            wb = [sb(s1, f"wb{i}", [128, 32 * 256], BF16) for i in range(2)]
            cos_sb = sb(s1, "cos_sb", [128, 512], F32)
            sin_sb = sb(s1, "sin_sb", [128, 512], F32)
            kraw = sb(s1, "kraw", [128, 1024], F32)
            kp = sb(s1, "kp", [128, 1024], F32)
            robuf = sb(s1, "robuf", [128, 1024], BF16)
            redt = sb(s1, "redt", [128, 1], F32)
            vout = sb(s1, "vout", [128, 256], BF16)
            sq16 = sb(s1, "sq16", [128, 16], F32)
            sq16b = sb(s1, "sq16b", [128, 16], BF16)
            kvisb = sb(s1, "kvisb", [128, 2 * 512], BF16)
            impf = sb(s1, "impf", [1, 512], F32)
            impB = sb(s1, "impB", [128, 512], F32)
            impP = sb(s1, "impP", [128, 4], F32)
            gt512 = sb(s1, "gt512", [128, 512], F32)
            rank4 = sb(s1, "rank4", [128, 4], F32)
            m768 = sb(s1, "m768", [1, 768], F32)

            xTv = xT.rearrange("p (k t) -> p k t", t=S)
            wbv = [w.rearrange("p (k w) -> p k w", w=256) for w in wb]

            ev_sq0 = P.op("vector", lambda v: v.memset(sq16[:], 0.0), inc=vs)

            ev_xt = []
            for c in range(4):
                ev_xt.append(P.dma(
                    "gpsimd", qg, xTv[:, :, 512 * c:512 * (c + 1)],
                    xT_in[:, 512 * c:512 * (c + 1)].rearrange(
                        "(k p) t -> p k t", p=128)))

            ev_wload = {}
            proj_evac = []
            ev_t2 = None
            ev_addk = None
            ev_robf = None
            ev_shift = None
            ev_sq = ev_sq0
            last_qs_store = None
            n_unit = 0
            v_unit = 0
            ev_vout_dma = None
            ev_vout_evac = []
            gi = 0

            for pi, wsrc in enumerate([qwT_in, kwT_in, vwT_in]):
                dst = [qT_d, kT_d, None][pi]
                for g in range(8):
                    waits = []
                    if gi >= 2:
                        waits.append((ts, ev_wload[gi - 2]))
                    ev_w = P.dma("gpsimd", qg, wbv[gi % 2][:, :, :],
                                 wsrc[:, 256 * g:256 * (g + 1)].rearrange(
                                     "(k p) w -> p k w", p=128), waits=waits)
                    mm_ev = None
                    if pi < 2:
                        for c in range(4):
                            cw = [(qg, ev_w), (qg, ev_xt[c])]
                            if ev_t2 is not None:
                                cw.append((vs, ev_t2))
                            evc = P.dma("sync", qs, cos_sb[:],
                                        cosD_in[:, 512 * c:512 * (c + 1)],
                                        waits=cw)
                            evc2 = P.dma("sync", qs, sin_sb[:],
                                         sinD_in[:, 512 * c:512 * (c + 1)])
                            for hh in range(2):
                                bank = PB[n_unit % 2]
                                mw = [(qg, ev_w), (qg, ev_xt[c])]
                                if n_unit >= 2:
                                    mw.append((ss, proj_evac[n_unit - 2]))
                                for k in range(32):
                                    def f(t, bank=bank, k=k, hh=hh, c=c,
                                          wv=wbv[gi % 2]):
                                        return t.matmul(
                                            bank[:],
                                            wv[:, k, 128 * hh:128 * (hh + 1)],
                                            xTv[:, k, 512 * c:512 * (c + 1)],
                                            start=(k == 0), stop=(k == 31))
                                    mm_ev = P.op("tensor", f,
                                                 waits=mw if k == 0 else (),
                                                 inc=ts if k == 31 else None)
                                ew = [(ts, mm_ev)]
                                if ev_addk is not None:
                                    ew.append((vs, ev_addk))
                                if ev_shift is not None:
                                    ew.append((qs, ev_shift))
                                eev = P.op(
                                    "scalar", lambda sc, bank=bank, hh=hh:
                                    sc.copy(kraw[:, 512 * hh:512 * (hh + 1)],
                                            bank[:]), waits=ew, inc=ss)
                                proj_evac.append(eev)
                                n_unit += 1
                            shw = [(ss, proj_evac[-1])]
                            if ev_robf is not None:
                                shw.append((vs, ev_robf))
                            P.dma("sync", qs, kp[0:64, :], kraw[64:128, :],
                                  waits=shw)
                            ev_shift = P.dma("sync", qs, kp[64:128, :],
                                             kraw[0:64, :])
                            k3 = kraw.rearrange("p (h t) -> p h t", t=512)
                            p3 = kp.rearrange("p (h t) -> p h t", t=512)
                            cb = cos_sb[:].unsqueeze(1).broadcast_to([128, 2, 512])
                            sbx = sin_sb[:].unsqueeze(1).broadcast_to([128, 2, 512])
                            P.op("vector", lambda v, k3=k3, cb=cb:
                                 v.tensor_tensor(k3, k3, cb, ALU.mult),
                                 waits=[(qs, evc2), (qs, ev_shift)])
                            ev_t2 = P.op("vector", lambda v, p3=p3, sbx=sbx:
                                         v.tensor_tensor(p3, p3, sbx, ALU.mult),
                                         inc=vs)
                            rw = []
                            if last_qs_store is not None:
                                rw.append((qs, last_qs_store))
                            r3 = robuf.rearrange("p (h t) -> p h t", t=512)
                            ev_addk = P.op("vector", lambda v, r3=r3, k3=k3, p3=p3:
                                           v.tensor_tensor(r3, k3, p3, ALU.add),
                                           waits=rw, inc=vs)
                            ev_robf = ev_addk
                            if pi == 0:
                                tlo = max(TXT_LO, 512 * c) - 512 * c
                                thi = min(TXT_HI, 512 * (c + 1)) - 512 * c
                                if thi > tlo:
                                    for hh in range(2):
                                        hcol = 2 * g + hh
                                        P.op("vector",
                                             lambda v, hh=hh, tlo=tlo, thi=thi:
                                             v.reduce_sum(
                                                 redt[:],
                                                 robuf[:, 512 * hh + tlo:
                                                       512 * hh + thi], AX))
                                        ev_sq = P.op(
                                            "vector", lambda v, hcol=hcol:
                                            v.tensor_tensor(
                                                sq16[:, hcol:hcol + 1],
                                                sq16[:, hcol:hcol + 1],
                                                redt[:], ALU.add), inc=vs)
                                    ev_robf = ev_sq
                            for hh in range(2):
                                h = 2 * g + hh
                                last_qs_store = P.dma(
                                    "sync", qs,
                                    dst[h, :, 512 * c:512 * (c + 1)],
                                    robuf[:, 512 * hh:512 * (hh + 1)],
                                    waits=[(vs, ev_addk)])
                    else:
                        for c in range(4):
                            for sub in range(4):
                                tt = 4 * c + sub
                                bank = PB[v_unit % 2]
                                mw = [(qg, ev_w), (qg, ev_xt[c])]
                                if v_unit >= 2:
                                    mw.append((ss, ev_vout_evac[v_unit - 2]))
                                for k in range(32):
                                    def f(t, bank=bank, k=k, tt=tt,
                                          wv=wbv[gi % 2]):
                                        return t.matmul(
                                            bank[:, 0:256],
                                            xTv[:, k, 128 * tt:128 * (tt + 1)],
                                            wv[:, k, :],
                                            start=(k == 0), stop=(k == 31))
                                    mm_ev = P.op("tensor", f,
                                                 waits=mw if k == 0 else (),
                                                 inc=ts if k == 31 else None)
                                ew = [(ts, mm_ev)]
                                if ev_vout_dma is not None:
                                    ew.append((qs, ev_vout_dma))
                                eev = P.op("scalar", lambda sc, bank=bank:
                                           sc.copy(vout[:], bank[:, 0:256]),
                                           waits=ew, inc=ss)
                                ev_vout_evac.append(eev)
                                ev_vout_dma = P.dma(
                                    "sync", qs,
                                    v_d[128 * tt:128 * (tt + 1),
                                        256 * g:256 * (g + 1)],
                                    vout[:], waits=[(ss, eev)])
                                v_unit += 1
                    ev_wload[gi] = mm_ev
                    gi += 1

                if pi == 1:
                    ev_k_st = last_qs_store
                    ev_sqb = P.op("vector",
                                  lambda v: v.tensor_copy(sq16b[:], sq16[:]),
                                  waits=[(vs, ev_sq)], inc=vs)
                    impev = None
                    ev_kv = {}
                    for h in range(HPC):
                        w = [(qs, ev_k_st)] if h == 0 else []
                        if h >= 2:
                            w.append((ts, ev_kv[h - 2]))
                        ev_l = P.dma("sync", qs,
                                     kvisb[:, 512 * (h % 2):512 * (h % 2 + 1)],
                                     kT_d[h, :, VIS_LO:VIS_HI], waits=w)
                        def f(t, h=h):
                            return t.matmul(
                                PB[4][0:1, :], sq16b[:, h:h + 1],
                                kvisb[:, 512 * (h % 2):512 * (h % 2 + 1)],
                                start=(h == 0), stop=(h == HPC - 1))
                        w2 = [(qs, ev_l)]
                        if h == 0:
                            w2.append((vs, ev_sqb))
                        impev = P.op("tensor", f, waits=w2,
                                     inc=ts if h == HPC - 1 else None)
                        ev_kv[h] = impev if h == HPC - 1 else P.op(
                            "tensor", lambda t: t.nop(), inc=ts)
                    ev_if = P.op("vector", lambda v: v.tensor_scalar(
                        impf[:], PB[4][0:1, :], flag_sb[0:1, 0:1], None,
                        ALU.mult), waits=[(ts, impev), (qs, ev_flag)], inc=vs)
                    ev_iw = P.dma("sync", qs, impi_d[:], impf[:],
                                  waits=[(vs, ev_if)])
                    ev_ar = P.op("gpsimd", lambda g: g.collective_compute(
                        "AllReduce", ALU.add, replica_groups=allg,
                        ins=[impi_d.opt()], outs=[impo_d.opt()]),
                        waits=[(qs, ev_iw)], inc=ccol)
                    ev_ir = P.dma("gpsimd", qg, impf[:], impo_d[:],
                                  waits=[(ccol, ev_ar)])
                    ev_ib = P.op("tensor", lambda t: t.matmul(
                        PB[5][:, :], ones_r[0:1, :], impf[0:1, :],
                        start=True, stop=True),
                        waits=[(qg, ev_ir), (vs, ev_if), (vs, ev_one)], inc=ts)
                    ev_ipm = None
                    for qq in range(4):
                        def f(t, qq=qq):
                            return t.matmul(PB[6][:, qq:qq + 1],
                                            impf[0:1, 128 * qq:128 * (qq + 1)],
                                            one1[0:1, 0:1],
                                            start=True, stop=True)
                        ev_ipm = P.op("tensor", f,
                                      inc=ts if qq == 3 else None)
                    ev_ibc = P.op("scalar", lambda sc: sc.copy(impB[:], PB[5][:]),
                                  waits=[(ts, ev_ib)], inc=ss)
                    ev_ipc = P.op("scalar",
                                  lambda sc: sc.copy(impP[:], PB[6][:, 0:4]),
                                  waits=[(ts, ev_ipm)], inc=ss)
                    for qq in range(4):
                        P.op("vector", lambda v, qq=qq: v.tensor_scalar(
                            gt512[:], impB[:], impP[:, qq:qq + 1], None,
                            ALU.is_gt),
                            waits=[(ss, ev_ipc), (ss, ev_ibc)])
                        P.op("vector", lambda v, qq=qq: v.reduce_sum(
                            rank4[:, qq:qq + 1], gt512[:], AX))
                    P.op("vector", lambda v: v.tensor_scalar(
                        rank4[:], rank4[:], 255.5, None, ALU.is_lt))
                    P.op("vector", lambda v: v.tensor_scalar(
                        rank4[:], rank4[:], 1.0e30, -1.0e30, ALU.mult, ALU.add))
                    ev_mv = P.op("vector", lambda v: v.tensor_scalar(
                        rank4[:], rank4[:], -20.0, None, ALU.add), inc=vs)
                    ev_m20 = P.op("vector", lambda v: v.memset(m768[:], -20.0),
                                  inc=vs)
                    ev_md = P.dma("sync", qs, mask_d[:], m768[:],
                                  waits=[(vs, ev_m20)])
                    ev_md2 = P.dma(
                        "sync", qs,
                        mask_d[0:1, 1:513].rearrange("a (q p) -> (a p) q", p=128),
                        rank4[:], waits=[(vs, ev_mv), (qs, ev_md)],
                        allow_slow_non_contiguous=True)
                    ev_mb = None
                    for j in range(5):
                        ev_mb = P.dma("sync", qs, maskb[:, j:j + 1],
                                      mask_d[0:1, 128 * j:128 * (j + 1)]
                                      .rearrange("a p -> p a"),
                                      waits=[(qs, ev_md2)])
                    MASK_READY = (qs, ev_mb)

            LAST_V = (qs, ev_vout_dma)
            P.emit_phase(block)

        with ExitStack() as s2:
            pvT = sb(s2, "pvT", [128, HPC * S], BF16)
            qbuf = [sb(s2, f"qb{i}", [128, 512], BF16) for i in range(2)]
            kbuf = [sb(s2, f"kb{i}", [128, 2048], BF16) for i in range(2)]
            vbuf = [sb(s2, f"vb{i}", [128, 16 * 132], BF16) for i in range(2)]
            pT = [sb(s2, f"pt{i}", [128, 512], BF16) for i in range(2)]
            pvn = sb(s2, "pvn", [128, 512], BF16)
            rcp = sb(s2, "rcp", [128, 4], F32)
            dq_pk = [sb(s2, f"dqp{i}", [128, 2048], I16) for i in range(2)]
            dq_sc = sb(s2, "dqs", [128, 2048], F32)
            dq_mk = sb(s2, "dqm", [128, 2048], F32)
            dq_lo = sb(s2, "dql", [128, 2048], I16)
            dq_w = [sb(s2, f"dqw{i}", [128, 2048], BF16) for i in range(2)]
            owT = [sb(s2, f"ow{i}", [128, 16 * 512], BF16) for i in range(2)]
            obuf = [sb(s2, f"ob{i}", [128, 512], BF16) for i in range(4)]

            bar = P.barrier_waits()
            P.op("vector", lambda v: v.memset(vbuf[0][:], 1.0), waits=bar)
            ev_vb = P.op("vector", lambda v: v.memset(vbuf[1][:], 1.0), inc=vs)

            dq_ev_v = {}
            dq_ev_st = {}

            def emit_dq_step(i):
                pt_i, ch = i // 2, i % 2
                cols = slice(2048 * ch, 2048 * (ch + 1))
                b = i % 2
                w_pk = list(bar) if i == 0 else []
                if i >= 2:
                    w_pk.append((vs, dq_ev_v[i - 2]))
                e_pk = P.dma("sync", qs, dq_pk[b][:],
                             pkT_in[128 * pt_i:128 * (pt_i + 1), cols],
                             waits=w_pk)
                w_sc = list(bar) if i == 0 else [(vs, dq_ev_v[i - 1])]
                e_sc = P.dma("gpsimd", qg, dq_sc[:],
                             scP_in[128 * pt_i:128 * (pt_i + 1), cols],
                             waits=w_sc)
                e_mk = P.dma("gpsimd", qg, dq_mk[:],
                             mkP_in[128 * pt_i:128 * (pt_i + 1), cols])
                P.op("vector", lambda v: v.tensor_scalar(
                    dq_mk[:], dq_mk[:], 0.5, 0.5, ALU.mult, ALU.add),
                    waits=[(qg, e_mk)])
                P.op("vector", lambda v: v.tensor_tensor(
                    dq_mk[:], dq_sc[:], dq_mk[:], ALU.mult),
                    waits=[(qg, e_sc)])
                w_w = [(qa, dq_ev_st[i - 2])] if i >= 2 else []
                P.op("vector", lambda v, b=b: v.tensor_scalar(
                    dq_lo[:], dq_pk[b][:], 15, None, ALU.bitwise_and),
                    waits=[(qs, e_pk)])
                P.op("vector", lambda v, b=b: v.tensor_scalar(
                    dq_pk[b][:], dq_pk[b][:], 4, None, ALU.logical_shift_right))
                P.op("vector", lambda v: v.scalar_tensor_tensor(
                    dq_w[0][:], dq_lo[:], -8.0, dq_mk[:], ALU.add, ALU.mult),
                    waits=w_w)
                ev_v = P.op("vector", lambda v, b=b: v.scalar_tensor_tensor(
                    dq_w[1][:], dq_pk[b][:], -8.0, dq_mk[:], ALU.add, ALU.mult),
                    inc=vs)
                dq_ev_v[i] = ev_v
                P.dma("scalar", qa, wdq_d[pt_i, :, cols], dq_w[0][:],
                      waits=[(vs, ev_v)])
                dq_ev_st[i] = P.dma("scalar", qa, wdq_d[16 + pt_i, :, cols],
                                    dq_w[1][:])

            units = [(h, c) for c in (3, 2, 1, 0) for h in range(HPC)]
            ev_qload = {}
            ev_kload = {}
            ev_vload = {}
            ev_unit_ts = {}
            g_att = {}
            g_exp = {}
            g_pv = {}
            ev_norm = {}
            ev_evac = {}
            step = 0
            dq_i = 0

            def issue_loads(idx):
                h, c = units[idx]
                b = idx % 2
                J = 4 * c + 4
                w = []
                if idx >= 2:
                    w.append((ts, ev_unit_ts[idx - 2]))
                else:
                    w.extend(bar)
                    w.append((vs, ev_vb))
                    w.append(LAST_V)
                ev_qload[idx] = P.dma("sync", qs, qbuf[b][:],
                                      qT_d[h, :, 512 * c:512 * (c + 1)],
                                      waits=w)
                ev_kload[idx] = P.dma("sync", qs, kbuf[b][:, 0:128 * J],
                                      kT_d[h, :, 0:128 * J])
                vv = vbuf[b].rearrange("p (j w) -> p j w", w=132)
                ev_vload[idx] = P.dma(
                    "sync", qs, vv[:, 0:J, 0:128],
                    v_d[0:128 * J, 128 * h:128 * (h + 1)].rearrange(
                        "(j p) w -> p j w", p=128))

            issue_loads(0)
            issue_loads(1)
            for idx, (h, c) in enumerate(units):
                b = idx % 2
                J = 4 * c + 4
                korder = (list(range(5, J)) + [0, 1, 2, 3, 4])[:J] if J > 4 \
                    else list(range(J))
                vv = vbuf[b].rearrange("p (j w) -> p j w", w=132)
                nk = len(korder)
                for jn, j in enumerate(korder):
                    ab = PB[step % 2]
                    aw = [(qs, ev_kload[idx]), (qs, ev_qload[idx])]
                    if step >= 2:
                        aw.append((ss, g_exp[step - 2]))
                    g_att[step] = P.op(
                        "tensor", lambda t, ab=ab, j=j, b=b:
                        t.matmul(ab[:], kbuf[b][:, 128 * j:128 * (j + 1)],
                                 qbuf[b][:], start=True, stop=True),
                        waits=aw, inc=ts)
                    d = j - 4 * c
                    isdiag = 0 <= d <= 3
                    col0 = 128 * d if isdiag else 0
                    ew = [(ts, g_att[step])]
                    if step >= 2:
                        ew.append((ts, g_pv[step - 2]))
                    if j <= 4:
                        ew.append(MASK_READY)
                    bias = maskb[:, j:j + 1] if j <= 4 else b20[:, 0:1]
                    g_exp[step] = P.op(
                        "scalar", lambda sc, sp=step, ab=ab, col0=col0,
                        bias=bias:
                        sc.activation(pT[sp % 2][:, col0:512], ab[:, col0:512],
                                      AF.Exp, bias=bias, scale=SCALE),
                        waits=ew, inc=ss)
                    ev_vx = None
                    if isdiag and d > 0:
                        P.op("vector", lambda v, sp=step, col0=col0:
                             v.memset(pT[sp % 2][:, 0:col0], 0.0),
                             waits=[(ss, g_exp[step])])
                    if isdiag:
                        ev_vx = P.op(
                            "vector", lambda v, sp=step, d=d:
                            v.tensor_tensor(
                                pT[sp % 2][:, 128 * d:128 * (d + 1)],
                                pT[sp % 2][:, 128 * d:128 * (d + 1)],
                                tri[:], ALU.mult),
                            waits=[(ss, g_exp[step]), (gs, ev_tri)], inc=vs)
                    pw = [(ss, g_exp[step]), (qs, ev_vload[idx])]
                    if ev_vx is not None:
                        pw.append((vs, ev_vx))
                    if jn == 0:
                        if idx >= 1:
                            pw.append((vs, ev_norm[idx - 1]))
                        else:
                            pw.append((vs, ev_if))
                            pw.append((ss, ev_ipc))
                    lastpv = None
                    for sq_ in range(4):
                        def f(t, sq_=sq_, sp=step, j=j, vv=vv, jn=jn, nk=nk):
                            return t.matmul(
                                PB[2 + sq_][:, 0:129],
                                pT[sp % 2][:, 128 * sq_:128 * (sq_ + 1)],
                                vv[:, j, 0:129],
                                start=(jn == 0), stop=(jn == nk - 1))
                        lastpv = P.op("tensor", f,
                                      waits=pw if sq_ == 0 else (),
                                      inc=ts if sq_ == 3 else None)
                    g_pv[step] = lastpv
                    step += 1
                nw = [(ts, g_pv[step - 1])]
                if idx >= 1:
                    nw.append((ts, ev_unit_ts[idx - 1]))
                ev_n = None
                for sq_ in range(4):
                    P.op("vector", lambda v, sq_=sq_: v.reciprocal(
                        rcp[:, sq_:sq_ + 1], PB[2 + sq_][:, 128:129]),
                        waits=nw if sq_ == 0 else ())
                    ev_n = P.op("vector", lambda v, sq_=sq_: v.tensor_scalar(
                        pvn[:, 128 * sq_:128 * (sq_ + 1)],
                        PB[2 + sq_][:, 0:128],
                        rcp[:, sq_:sq_ + 1], None, ALU.mult),
                        inc=vs if sq_ == 3 else None)
                ev_norm[idx] = ev_n
                trw = [(vs, ev_n)]
                if idx >= 1:
                    trw.append((ss, ev_evac[idx - 1]))
                else:
                    trw.append((ss, ev_ipc))
                ev_tr = None
                for sq_ in range(4):
                    ev_tr = P.op(
                        "tensor", lambda t, sq_=sq_:
                        t.transpose(PBt[:, 128 * sq_:128 * (sq_ + 1)],
                                    pvn[:, 128 * sq_:128 * (sq_ + 1)],
                                    ident[:]),
                        waits=trw if sq_ == 0 else (),
                        inc=ts if sq_ == 3 else None)
                ev_unit_ts[idx] = ev_tr
                ev_evac[idx] = P.op(
                    "scalar", lambda sc, h=h, c=c:
                    sc.copy(pvT[:, S * h + 512 * c:S * h + 512 * (c + 1)],
                            PBt[:, 0:512]),
                    waits=[(ts, ev_tr)], inc=ss)
                if idx + 2 < len(units):
                    issue_loads(idx + 2)
                if idx % 2 == 1 and dq_i < 32:
                    emit_dq_step(dq_i)
                    dq_i += 1
            while dq_i < 32:
                emit_dq_step(dq_i)
                dq_i += 1

            ATT_DONE = (ss, ev_evac[len(units) - 1])
            LAST_NORM = (vs, ev_norm[len(units) - 1])

            ev_ow = {}
            obank_evac = {}
            ev_rsw = {}
            rs_done = {}
            oi = 0
            for half, cgrp in enumerate([(0, 2), (1, 3)]):
                for fc in range(8):
                    bo = oi % 2
                    w = []
                    if oi >= 2:
                        w.append((ts, ev_ow[oi - 2]))
                    e_ow = P.dma("gpsimd", qg,
                                 owT[bo].rearrange("p (k w) -> p k w", w=512),
                                 owT_in[:, 512 * fc:512 * (fc + 1)].rearrange(
                                     "(k p) w -> p k w", p=128), waits=w)
                    mm_ev = None
                    for ci in cgrp:
                        for sq_ in range(4):
                            bank = PB[sq_]
                            mw = [(qg, e_ow), ATT_DONE]
                            if sq_ >= 2:
                                mw.append(LAST_NORM)
                            pr = obank_evac.get(sq_)
                            if pr is not None:
                                mw.append((ss, pr))
                            for k in range(16):
                                def f(t, bank=bank, k=k, ci=ci, sq_=sq_, bo=bo):
                                    ow_ = owT[bo].rearrange(
                                        "p (kk w) -> p kk w", w=512)
                                    return t.matmul(
                                        bank[:],
                                        pvT[:, S * k + 512 * ci + 128 * sq_:
                                            S * k + 512 * ci + 128 * (sq_ + 1)],
                                        ow_[:, k, :],
                                        start=(k == 0), stop=(k == 15))
                                mm_ev = P.op("tensor", f,
                                             waits=mw if k == 0 else (),
                                             inc=ts if k == 15 else None)
                            eev = P.op("scalar",
                                       lambda sc, bank=bank, sq_=sq_:
                                       sc.copy(obuf[sq_][:], bank[:]),
                                       waits=[(ts, mm_ev)], inc=ss)
                            obank_evac[sq_] = eev
                            t0 = 512 * ci + 128 * sq_
                            r0 = (512 * ((t0 % 1024) // 256)
                                  + 256 * (t0 // 1024) + (t0 % 256))
                            ev_rsw[(ci, sq_, fc)] = P.dma(
                                "scalar", qa,
                                rsi_d[r0:r0 + 128, 512 * fc:512 * (fc + 1)],
                                obuf[sq_][:], waits=[(ss, eev)])
                    ev_ow[oi] = mm_ev
                    oi += 1
                for g in (2 * half, 2 * half + 1):
                    w = []
                    for ci in cgrp:
                        for sq_ in range(4):
                            t0 = 512 * ci + 128 * sq_
                            r0 = (512 * ((t0 % 1024) // 256)
                                  + 256 * (t0 // 1024))
                            if 512 * g <= r0 < 512 * (g + 1):
                                w.append((qa, ev_rsw[(ci, sq_, 7)]))
                    rs_done[g] = P.op(
                        "gpsimd", lambda gp, g=g: gp.collective_compute(
                            "ReduceScatter", ALU.add, replica_groups=pairs,
                            ins=[rsi_d[512 * g:512 * (g + 1), :].opt()],
                            outs=[rso_d[256 * g:256 * (g + 1), :].opt()]),
                        waits=w, inc=ccol)
            P.emit_phase(block)

        with ExitStack() as s3:
            a_sb = sb(s3, "a_sb", [128, 8 * DM], BF16)
            bias_sb = sb(s3, "bias_sb", [128, DM], F32)
            ssum = sb(s3, "ssum", [128, 4 * 16], F32)
            ssq = sb(s3, "ssq", [128, 4 * 16], F32)

            bar2 = P.barrier_waits()
            ev_bias = P.dma("sync", qs, bias_sb[:], biasr_in[:], waits=bar2)

            ev_a = {}
            with ExitStack() as s3a:
                osum = sb(s3a, "osum", [128, DM], BF16)
                xres = sb(s3a, "xres_t", [128, DM], F32)
                r1 = sb(s3a, "r1", [128, DM], F32)
                prev_r1 = None
                prev_sq = None
                prev_xr = None
                for tt in range(8):
                    g = tt // 2
                    w = [(ccol, rs_done[g])] + list(bar2)
                    if prev_sq is not None:
                        w.append((vs, prev_sq))
                    e_os = P.dma("sync", qs, osum[:],
                                 rso_d[128 * tt:128 * (tt + 1), :], waits=w)
                    w2 = [(vs, prev_xr)] if prev_xr is not None else []
                    e_xr = P.dma("sync", qs, xres[:],
                                 xres_in[128 * tt:128 * (tt + 1), :], waits=w2)
                    rw = [(qs, e_xr), (qs, e_os)]
                    if prev_r1 is not None:
                        rw.append((vs, prev_r1))
                    prev_xr = P.op("vector", lambda v: v.scalar_tensor_tensor(
                        r1[:], osum[:], 1.0, xres[:], ALU.mult, ALU.add,
                        accum_out=stat[:, 0:1]), waits=rw, inc=vs)
                    prev_sq = P.op("vector", lambda v: v.scalar_tensor_tensor(
                        osum[:], r1[:], 1.0, r1[:], ALU.mult, ALU.mult,
                        accum_out=stat[:, 1:2]), inc=vs)
                    P.op("vector", lambda v: v.tensor_scalar(
                        stat[:, 2:3], stat[:, 0:1], 1.0 / DM, None, ALU.mult))
                    P.op("vector", lambda v: v.tensor_scalar(
                        stat[:, 3:4], stat[:, 1:2], 1.0 / DM, None, ALU.mult))
                    P.op("vector", lambda v: v.tensor_tensor(
                        stat[:, 4:5], stat[:, 2:3], stat[:, 2:3], ALU.mult))
                    ev_var = P.op("vector", lambda v: v.tensor_tensor(
                        stat[:, 5:6], stat[:, 3:4], stat[:, 4:5],
                        ALU.subtract), inc=vs)
                    P.op("scalar", lambda sc: sc.activation(
                        stat[:, 6:7], stat[:, 5:6], AF.Ln, bias=beps[:, 0:1]),
                        waits=[(vs, ev_var)])
                    ev_rstd = P.op("scalar", lambda sc: sc.activation(
                        stat[:, 7:8], stat[:, 6:7], AF.Exp, scale=-0.5),
                        inc=ss)
                    prev_r1 = P.op("vector", lambda v, tt=tt: v.tensor_scalar(
                        a_sb[:, DM * tt:DM * (tt + 1)], r1[:],
                        stat[:, 2:3], stat[:, 7:8], ALU.subtract, ALU.mult),
                        waits=[(ss, ev_rstd)], inc=vs)
                    ev_a[tt] = prev_r1
            P4_DONE = (vs, ev_a[7])
            if dbg:
                evd = P.dma("sync", qs, dbg1_d[:], a_sb[:], waits=[P4_DONE])
                P4_DONE = (qs, evd)

            with ExitStack() as s3b:
                aT = sb(s3b, "aT", [128, 32 * 512], BF16)
                wsl = [sb(s3b, f"ws{i}", [128, 32 * 256], BF16)
                       for i in range(2)]
                tbuf = sb(s3b, "tbuf", [128, 256], F32)
                j256 = sb(s3b, "j256", [128, 256], F32)
                outb = sb(s3b, "outb", [128, 256], F32)

                last_out = None
                ev_wsl = {}
                ev_fev = {}
                ev_trev = None
                wi = 0
                for grp in range(2):
                    for it in range(32):
                        ev_tr = None
                        for tl in range(4):
                            tt = 4 * grp + tl
                            w = [P4_DONE]
                            if ev_trev is not None:
                                w.append((ss, ev_trev))
                            ev_tr = P.op(
                                "tensor", lambda t, it=it, tl=tl, tt=tt:
                                t.transpose(PBt[:, 128 * tl:128 * (tl + 1)],
                                            a_sb[:, DM * tt + 128 * it:
                                                 DM * tt + 128 * (it + 1)],
                                            ident[:]),
                                waits=w if tl == 0 else (),
                                inc=ts if tl == 3 else None)
                        ev_trev = P.op("scalar", lambda sc, it=it:
                                       sc.copy(aT[:, 512 * it:512 * (it + 1)],
                                               PBt[:, 0:512]),
                                       waits=[(ts, ev_tr)], inc=ss)
                    AT_RDY = (ss, ev_trev)
                    if dbg and grp == 0:
                        evd2 = P.dma("sync", qs, dbg2_d[:], aT[:],
                                     waits=[AT_RDY])
                        AT_RDY = (qs, evd2)
                    for oc in range(16):
                        bw = oc % 2
                        w = [AT_RDY, P4_DONE, (qa, dq_ev_st[31])]
                        if wi >= 2:
                            w.append((ts, ev_wsl[wi - 2]))
                        e_w = P.dma("sync", qs,
                                    wsl[bw].rearrange("p (k w) -> p k w",
                                                      w=256),
                                    wdq_d[:, :, 256 * oc:256 * (oc + 1)]
                                    .transpose([1, 0, 2]), waits=w)
                        mm_ev = None
                        for k in range(32):
                            for sq_ in range(4):
                                bank = PB[sq_]
                                mw = []
                                if k == 0:
                                    mw = [(qs, e_w), AT_RDY]
                                    pr = ev_fev.get((wi - 1, sq_))
                                    if pr is not None:
                                        mw.append((vs, pr))
                                def f(t, bank=bank, k=k, sq_=sq_, bw=bw):
                                    wv_ = wsl[bw].rearrange(
                                        "p (kk w) -> p kk w", w=256)
                                    return t.matmul(
                                        bank[:, 0:256],
                                        aT[:, 512 * k + 128 * sq_:
                                           512 * k + 128 * (sq_ + 1)],
                                        wv_[:, k, :],
                                        start=(k == 0), stop=(k == 31))
                                mm_ev = P.op(
                                    "tensor", f, waits=mw,
                                    inc=ts if (k == 31 and sq_ == 3) else None)
                        ev_wsl[wi] = mm_ev
                        for sq_ in range(4):
                            tt = 4 * grp + sq_
                            asl = a_sb[:, DM * tt + 256 * oc:
                                       DM * tt + 256 * (oc + 1)]
                            P.op("vector", lambda v, sq_=sq_, oc=oc:
                                 v.tensor_tensor(
                                     tbuf[:], PB[sq_][:, 0:256],
                                     bias_sb[:, 256 * oc:256 * (oc + 1)],
                                     ALU.add),
                                 waits=[(ts, mm_ev), (qs, ev_bias)])
                            P.op("vector", lambda v, asl=asl, sq_=sq_, oc=oc:
                                 v.scalar_tensor_tensor(
                                     asl, tbuf[:], 1.0, asl, ALU.mult,
                                     ALU.add,
                                     accum_out=ssum[:, 16 * sq_ + oc:
                                                    16 * sq_ + oc + 1]))
                            ev_fev[(wi, sq_)] = P.op(
                                "vector", lambda v, asl=asl, sq_=sq_, oc=oc:
                                v.scalar_tensor_tensor(
                                    j256[:], asl, 1.0, asl, ALU.mult, ALU.mult,
                                    accum_out=ssq[:, 16 * sq_ + oc:
                                                  16 * sq_ + oc + 1]),
                                inc=vs)
                        wi += 1
                    for sq_ in range(4):
                        tt = 4 * grp + sq_
                        w = [(vs, ev_fev[(wi - 1, sq_)])]
                        P.op("vector", lambda v, sq_=sq_: v.reduce_sum(
                            stat[:, 0:1], ssum[:, 16 * sq_:16 * (sq_ + 1)],
                            AX), waits=w)
                        P.op("vector", lambda v, sq_=sq_: v.reduce_sum(
                            stat[:, 1:2], ssq[:, 16 * sq_:16 * (sq_ + 1)],
                            AX))
                        P.op("vector", lambda v: v.tensor_scalar(
                            stat[:, 2:3], stat[:, 0:1], 1.0 / DM, None,
                            ALU.mult))
                        P.op("vector", lambda v: v.tensor_scalar(
                            stat[:, 3:4], stat[:, 1:2], 1.0 / DM, None,
                            ALU.mult))
                        P.op("vector", lambda v: v.tensor_tensor(
                            stat[:, 4:5], stat[:, 2:3], stat[:, 2:3],
                            ALU.mult))
                        ev_var = P.op("vector", lambda v: v.tensor_tensor(
                            stat[:, 5:6], stat[:, 3:4], stat[:, 4:5],
                            ALU.subtract), inc=vs)
                        P.op("scalar", lambda sc: sc.activation(
                            stat[:, 6:7], stat[:, 5:6], AF.Ln, bias=beps[:, 0:1]),
                            waits=[(vs, ev_var)])
                        ev_rstd = P.op("scalar", lambda sc: sc.activation(
                            stat[:, 7:8], stat[:, 6:7], AF.Exp, scale=-0.5),
                            inc=ss)
                        for oc in range(16):
                            ow = [(ss, ev_rstd)]
                            if last_out is not None:
                                ow.append((qa, last_out))
                            ev_o = P.op(
                                "vector", lambda v, tt=tt, oc=oc:
                                v.tensor_scalar(
                                    outb[:],
                                    a_sb[:, DM * tt + 256 * oc:
                                         DM * tt + 256 * (oc + 1)],
                                    stat[:, 2:3], stat[:, 7:8],
                                    ALU.subtract, ALU.mult),
                                waits=ow, inc=vs)
                            last_out = P.dma(
                                "scalar", qa,
                                out_ext[128 * tt:128 * (tt + 1),
                                        256 * oc:256 * (oc + 1)],
                                outb[:], waits=[(vs, ev_o)])
                if dbg:
                    P.dma("sync", qs, dbg3_d[:], a_sb[:],
                          waits=[(qa, last_out)])
                if dbg == 2:
                    P.dma("sync", qs, dbg_qT[:], qT_d[:],
                          waits=[(qa, last_out)])
                    P.dma("sync", qs, dbg_kT[:], kT_d[:])
                    P.dma("sync", qs, dbg_v[:], v_d[:])
                    P.dma("sync", qs, dbg_w[:], wdq_d[:])
                    P.dma("sync", qs, dbg_rsi[:], rsi_d[:])
                    P.dma("sync", qs, dbg_rso[:], rso_d[:])
                    ev_dm = P.dma("sync", qs, dbg_msk[:], mask_d[:])
                fin = qa.n
                fin_qs = qs.n
                P.op("scalar", lambda sc: sc.nop(),
                     waits=[(qa, fin), (qs, fin_qs)])
            P.emit_phase(block)

    return nc


def _rope_tables():
    inv_freq = 1.0 / (10000.0 ** (np.arange(0, D, 2, dtype=np.float64) / D))
    freqs = np.arange(S, dtype=np.float64)[:, None] * inv_freq[None, :]
    emb = np.concatenate([freqs, freqs], axis=-1)
    cos = np.ascontiguousarray(np.cos(emb).T.astype(np.float32))
    sin = np.sin(emb).T.astype(np.float32)
    sgn = np.ones((D, 1), np.float32)
    sgn[:64] = -1.0
    return cos, np.ascontiguousarray(sin * sgn)


def kernel(L0_attn_hidden_states, L0_attn_q_w, L0_attn_k_w, L0_attn_v_w,
           L0_attn_o_w, L0_ffn_weight_packed, L0_ffn_weight_scale,
           L0_ffn_weight_var_mask, L0_ffn_bias):
    x = np.asarray(L0_attn_hidden_states)
    qw = np.asarray(L0_attn_q_w)
    kw = np.asarray(L0_attn_k_w)
    vw = np.asarray(L0_attn_v_w)
    ow = np.asarray(L0_attn_o_w)
    pk = np.asarray(L0_ffn_weight_packed)
    sc = np.asarray(L0_ffn_weight_scale)
    mk = np.asarray(L0_ffn_weight_var_mask)
    bias = np.asarray(L0_ffn_bias)

    perm = np.concatenate([np.arange(0, DM, 2), np.arange(1, DM, 2)])
    cos, sin = _rope_tables()

    p16 = np.ascontiguousarray(pk.view(np.int16)[:, 0::2])
    p16 = p16.reshape(DM, 128, 16).transpose(1, 2, 0).reshape(2048, DM)
    pkT = np.ascontiguousarray(p16[:, perm])
    scP = np.ascontiguousarray(np.repeat(sc.reshape(DM, 128)[perm].T, 16,
                                         axis=0))
    mkP = np.ascontiguousarray(np.repeat(mk.reshape(DM, 128)[perm].T
                                         .view(np.uint8), 16, axis=0))
    biasr = np.ascontiguousarray(np.broadcast_to(bias[perm][None, :],
                                                 (128, DM)))

    in_maps = []
    for c in range(8):
        b, hh = c // 2, c % 2
        rows = slice(2048 * hh, 2048 * (hh + 1))
        in_maps.append({
            "xT": np.ascontiguousarray(x[b].T),
            "xres": np.ascontiguousarray(
                x[b, 1024 * hh:1024 * (hh + 1)][:, perm]),
            "qwT": np.ascontiguousarray(qw[rows].T),
            "kwT": np.ascontiguousarray(kw[rows].T),
            "vwT": np.ascontiguousarray(vw[rows].T),
            "owT": np.ascontiguousarray(ow[perm][:, rows].T),
            "pkT": pkT, "scP": scP, "mkP": mkP, "biasr": biasr,
            "cosD": cos, "sinD": sin,
            "flag": np.array([[1.0 if c < 2 else 0.0]], np.float32),
        })

    nc = build()
    res = run_bass_kernel_spmd(nc, in_maps, core_ids=list(range(8)))

    out = np.empty((4, S, DM), np.float32)
    inv = np.empty(DM, np.int64)
    inv[perm] = np.arange(DM)
    for c in range(8):
        b, hh = c // 2, c % 2
        out[b, 1024 * hh:1024 * (hh + 1)] = res.results[c]["out"][:, inv]
    return out
